# version 04
# baseline (speedup 1.0000x reference)
import numpy as np
import concourse.bass as bass
import concourse.mybir as mybir
from concourse.bass_utils import run_bass_kernel_spmd
from contextlib import ExitStack

F32 = mybir.dt.float32
BF16 = mybir.dt.bfloat16
I32 = mybir.dt.int32
F8 = mybir.dt.float8e4
AF = mybir.ActivationFunctionType
OP = mybir.AluOpType
AX = mybir.AxisListType

P = 128
D = 2048
FI = 6144
T = 2048
W = 256
TW = T + W
NDC = D // P
NIC = FI // P
NTG = TW // P
NTGM = T // P
EPS = 1e-5
QCH = 512
GUB = 2


def build_program():
    nc = bass.Bass("TRN2", target_bir_lowering=False)

    x_in = nc.dram_tensor("x_sh", [TW, D], F32, kind="ExternalInput").ap()
    Wi = nc.dram_tensor("Wi", [D, D], F32, kind="ExternalInput").ap()
    Wf = nc.dram_tensor("Wf", [D, D], F32, kind="ExternalInput").ap()
    Wg = nc.dram_tensor("Wg", [D, D], F32, kind="ExternalInput").ap()
    Wo = nc.dram_tensor("Wo", [D, D], F32, kind="ExternalInput").ap()
    gnw = nc.dram_tensor("gnw", [P, NDC], F32, kind="ExternalInput").ap()
    gw = nc.dram_tensor("gw", [D, FI], F32, kind="ExternalInput").ap()
    uw = nc.dram_tensor("uw", [D, FI], F32, kind="ExternalInput").ap()
    dw = nc.dram_tensor("dw", [FI, D], F32, kind="ExternalInput").ap()
    gs_in = nc.dram_tensor("gs", [P, NIC], F32, kind="ExternalInput").ap()
    us_in = nc.dram_tensor("us", [P, NIC], F32, kind="ExternalInput").ap()
    ds_in = nc.dram_tensor("ds", [P, NDC], F32, kind="ExternalInput").ap()
    out_d = nc.dram_tensor("out_sh", [T, D], F32, kind="ExternalOutput").ap()

    scr_g = nc.dram_tensor("scr_g", [NIC, P, D], F8, kind="ExternalOutput").ap()
    scr_u = nc.dram_tensor("scr_u", [NIC, P, D], F8, kind="ExternalOutput").ap()
    scr_d = nc.dram_tensor("scr_d", [NDC, P, FI], F8, kind="ExternalOutput").ap()
    scr_s = nc.dram_tensor("scr_s", [D, T], BF16, kind="ExternalOutput").ap()
    scr_ao = nc.dram_tensor("scr_ao", [D, T], BF16, kind="ExternalOutput").ap()

    es_top = ExitStack()
    s_sy = es_top.enter_context(nc.semaphore("s_sy"))
    s_gp = es_top.enter_context(nc.semaphore("s_gp"))
    s_tn = es_top.enter_context(nc.semaphore("s_tn"))
    s_ve = es_top.enter_context(nc.semaphore("s_ve"))
    s_se = es_top.enter_context(nc.semaphore("s_se"))
    s_ct = es_top.enter_context(nc.semaphore("s_ct"))

    class Ctr:
        def __init__(self):
            self.v = 0

        def inc(self, n=1):
            self.v += n
            return self.v

    c_sy, c_gp, c_tn, c_ve, c_se, c_ct = Ctr(), Ctr(), Ctr(), Ctr(), Ctr(), Ctr()
    ops = {e: [] for e in ("sync", "gpsimd", "tensor", "vector", "scalar")}

    def em(eng, fn):
        ops[eng].append(fn)

    sb = lambda n, s, d: es_top.enter_context(nc.sbuf_tensor(n, s, d))
    xn_m = sb("xn_m", [P, NDC * T], BF16)
    xn_w = sb("xn_w", [P, NDC * W], BF16)
    ident = sb("ident", [P, P], F32)
    ones_c = sb("ones_c", [P, 1], BF16)
    ones_r = sb("ones_r", [1, P], F32)
    gam = sb("gam", [P, NDC], F32)
    gssb = sb("gssb", [P, NIC], F32)
    ussb = sb("ussb", [P, NIC], F32)
    dssb = sb("dssb", [P, NDC], F32)
    stat = sb("stat", [P, 8 * NTG], F32)

    psb = [es_top.enter_context(nc.psum_tensor(f"ps{i}", [P, 512], F32))
           for i in range(8)]

    def g(e):
        e.memset(ident[:], 0.0)
        e.affine_select(out=ident[:], in_=ident[:], compare_op=OP.not_equal,
                        fill=1.0, base=0, pattern=[[-1, P]], channel_multiplier=1)
        e.memset(ones_c[:], 1.0)
        e.memset(ones_r[:], 1.0).then_inc(s_ct, 1)
    em("gpsimd", g)
    CONST_CT = c_ct.inc()

    def s(e):
        e.dma_start(out=gam[:], in_=gnw[:]).then_inc(s_sy, 16)
        e.dma_start(out=gssb[:], in_=gs_in[:]).then_inc(s_sy, 16)
        e.dma_start(out=ussb[:], in_=us_in[:]).then_inc(s_sy, 16)
        e.dma_start(out=dssb[:], in_=ds_in[:]).then_inc(s_sy, 16)
    em("sync", s)
    CONST_SY = c_sy.inc(4 * 16)

    es_p0 = ExitStack()
    x_t = [es_p0.enter_context(nc.sbuf_tensor(f"x_t{i}", [P, D], F32)) for i in range(2)]
    xn_tok = es_p0.enter_context(nc.sbuf_tensor("xn_tok", [P, D], F32))

    xsum = stat[:, 0 * NTG:1 * NTG]
    xss = stat[:, 1 * NTG:2 * NTG]
    m_st = stat[:, 2 * NTG:3 * NTG]
    rstd = stat[:, 3 * NTG:4 * NTG]
    mrstd = stat[:, 4 * NTG:5 * NTG]
    ex2 = stat[:, 5 * NTG:6 * NTG]
    varv = stat[:, 6 * NTG:7 * NTG]
    lnv = stat[:, 7 * NTG:8 * NTG]

    p0a_ld, p0a_ve, p0a_se = [], [], []
    for tg in range(NTG):
        par = tg % 2

        def s(e, tg=tg, par=par):
            if tg >= 2:
                e.wait_ge(s_ve, p0a_ve[tg - 2])
                e.wait_ge(s_se, p0a_se[tg - 2])
            e.dma_start(out=x_t[par][:], in_=x_in[tg * P:(tg + 1) * P, :]).then_inc(s_sy, 16)
        em("sync", s)
        p0a_ld.append(c_sy.inc(16))

        def v(e, tg=tg, par=par):
            e.wait_ge(s_sy, p0a_ld[tg])
            e.reduce_sum(xsum[:, tg:tg + 1], x_t[par][:], axis=AX.X).then_inc(s_ve, 1)
        em("vector", v)
        p0a_ve.append(c_ve.inc())

        def sc(e, tg=tg, par=par):
            e.wait_ge(s_sy, p0a_ld[tg])
            e.activation(xn_tok[:], x_t[par][:], AF.Square,
                         accum_out=xss[:, tg:tg + 1]).then_inc(s_se, 1)
        em("scalar", sc)
        p0a_se.append(c_se.inc())

    def v(e):
        e.wait_ge(s_se, p0a_se[-1])
        e.tensor_scalar(m_st[:], xsum[:], 1.0 / D, None, OP.mult)
        e.tensor_scalar(ex2[:], xss[:], 1.0 / D, EPS, OP.mult, OP.add).then_inc(s_ve, 1)
    em("vector", v)
    VA1 = c_ve.inc()

    def v(e):
        e.wait_ge(s_ve, VA1)
        e.tensor_tensor(mrstd[:], m_st[:], m_st[:], OP.mult).then_inc(s_ve, 1)
    em("vector", v)
    VA2 = c_ve.inc()

    def v(e):
        e.wait_ge(s_ve, VA2)
        e.tensor_tensor(varv[:], ex2[:], mrstd[:], OP.subtract).then_inc(s_ve, 1)
    em("vector", v)
    VAR_VE = c_ve.inc()

    def sc(e):
        e.wait_ge(s_ve, VAR_VE)
        e.activation(lnv[:], varv[:], AF.Ln).then_inc(s_se, 1)
    em("scalar", sc)
    LN_SE = c_se.inc()

    def sc(e):
        e.wait_ge(s_se, LN_SE)
        e.activation(rstd[:], lnv[:], AF.Exp, scale=-0.5).then_inc(s_se, 1)
    em("scalar", sc)
    SQRT_SE = c_se.inc()

    def v(e):
        e.wait_ge(s_se, SQRT_SE)
        e.tensor_tensor(mrstd[:], m_st[:], rstd[:], OP.mult).then_inc(s_ve, 1)
    em("vector", v)
    RSTD_VE = c_ve.inc()

    p0b_ld, p0b_nrm, p0b_evac, p0b_tr = [], [], [], []
    p0b_trlast = []
    for tg in range(NTG):
        par = tg % 2

        def s(e, tg=tg, par=par):
            if tg < 2:
                e.wait_ge(s_ve, p0a_ve[NTG - 2 + tg])
                e.wait_ge(s_se, p0a_se[NTG - 2 + tg])
            else:
                e.wait_ge(s_ve, p0b_nrm[tg - 2])
            e.dma_start(out=x_t[par][:], in_=x_in[tg * P:(tg + 1) * P, :]).then_inc(s_sy, 16)
        em("sync", s)
        p0b_ld.append(c_sy.inc(16))

        def v(e, tg=tg, par=par):
            e.wait_ge(s_sy, p0b_ld[tg])
            if tg == 0:
                e.wait_ge(s_ve, RSTD_VE)
            if tg >= 1:
                e.wait_ge(s_tn, p0b_trlast[tg - 1])
            e.tensor_scalar(xn_tok[:], x_t[par][:], rstd[:, tg:tg + 1],
                            mrstd[:, tg:tg + 1], OP.mult, OP.subtract).then_inc(s_ve, 1)
        em("vector", v)
        p0b_nrm.append(c_ve.inc())

        for dc in range(NDC):
            gi = tg * NDC + dc
            bank = gi % 4

            def t(e, tg=tg, dc=dc, bank=bank, gi=gi):
                if gi == 0:
                    e.wait_ge(s_ct, CONST_CT)
                e.wait_ge(s_ve, p0b_nrm[tg])
                if gi >= 4:
                    e.wait_ge(s_ve, p0b_evac[gi - 4])
                e.transpose(psb[bank][:, 0:P], xn_tok[:, dc * P:(dc + 1) * P],
                            ident[:]).then_inc(s_tn, 1)
            em("tensor", t)
            p0b_tr.append(c_tn.inc())

            def v(e, tg=tg, dc=dc, bank=bank, gi=gi):
                e.wait_ge(s_tn, p0b_tr[gi])
                if tg < 2:
                    dst = xn_w[:, dc * W + tg * P: dc * W + (tg + 1) * P]
                else:
                    dst = xn_m[:, dc * T + (tg - 2) * P: dc * T + (tg - 1) * P]
                e.tensor_copy(dst, psb[bank][:, 0:P]).then_inc(s_ve, 1)
            em("vector", v)
            p0b_evac.append(c_ve.inc())
        p0b_trlast.append(p0b_tr[-1])

    P0_VE = c_ve.v
    P0_SE = c_se.v
    P0_TN = c_tn.v

    es_p0.close()
    es_q = ExitStack()
    qf32 = [es_q.enter_context(nc.sbuf_tensor(f"qf32{i}", [P, QCH], F32)) for i in range(2)]
    qf8 = [es_q.enter_context(nc.sbuf_tensor(f"qf8{i}", [P, QCH], F8)) for i in range(2)]
    qki = [es_q.enter_context(nc.sbuf_tensor(f"qki{i}", [P, QCH], I32)) for i in range(2)]
    qq = [es_q.enter_context(nc.sbuf_tensor(f"qq{i}", [P, QCH], F8)) for i in range(2)]

    scr_g_v = scr_g.rearrange("ic p (dc e) -> p ic dc e", e=P)
    scr_u_v = scr_u.rearrange("ic p (dc e) -> p ic dc e", e=P)
    scr_d_v = scr_d.rearrange("hc p (ic e) -> p hc ic e", e=P)

    q_list = []
    nb = QCH // P
    for mat, scr_v in ((gw, scr_g_v), (uw, scr_u_v)):
        for dc in range(NDC):
            for s4 in range(FI // QCH):
                q_list.append((mat[dc * P:(dc + 1) * P, s4 * QCH:(s4 + 1) * QCH],
                               scr_v[:, s4 * nb:(s4 + 1) * nb, dc, :]))
    for ic in range(NIC):
        for s4 in range(D // QCH):
            q_list.append((dw[ic * P:(ic + 1) * P, s4 * QCH:(s4 + 1) * QCH],
                           scr_d_v[:, s4 * nb:(s4 + 1) * nb, ic, :]))
    NQ = len(q_list)

    q_ld, q_p1, q_p2, q_wr = {}, {}, {}, {}

    class QEmit:
        def __init__(self):
            self.n = 0

        def emit(self, k):
            for _ in range(k):
                n = self.n
                if n >= NQ:
                    return
                self.n += 1
                src, dst = q_list[n]
                par = n % 2

                def s(e, n=n, par=par, src=src):
                    if n == 0:
                        e.wait_ge(s_ve, P0_VE)
                        e.wait_ge(s_tn, P0_TN)
                    if n >= 2:
                        e.wait_ge(s_ve, q_p1[n - 2])
                    e.dma_start(out=qf32[par][:], in_=src).then_inc(s_sy, 16)
                em("sync", s)
                q_ld[n] = c_sy.inc(16)

                def v(e, n=n, par=par, c1=c_ve.v + 1, c2=c_ve.v + 2):
                    e.wait_ge(s_sy, q_ld[n])
                    e.tensor_copy(qf8[par][:], qf32[par][:]).then_inc(s_ve, 1)
                    e.wait_ge(s_ve, c1)
                    e.tensor_scalar(qki[par][:], qf8[par][:], 512.0, None,
                                    OP.mult).then_inc(s_ve, 1)
                    if n >= 2:
                        e.wait_ge(s_sy, q_wr[n - 2])
                    e.wait_ge(s_ve, c2)
                    e.tensor_scalar(qq[par][:], qki[par][:], 2.0 ** -9, None,
                                    OP.mult).then_inc(s_ve, 1)
                em("vector", v)
                q_p1[n] = c_ve.inc()
                c_ve.inc()
                q_p2[n] = c_ve.inc()

                def g(e, n=n, par=par, dst=dst):
                    e.wait_ge(s_ve, q_p2[n])
                    e.dma_start(out=dst, in_=qq[par][:]).then_inc(s_sy, 16)
                em("sync", g)
                q_wr[n] = c_sy.inc(16)

        def finish(self):
            self.emit(NQ - self.n)

    qe = QEmit()

    es_a = ExitStack()
    asb = lambda n, s, d: es_a.enter_context(nc.sbuf_tensor(n, s, d))
    ho2 = asb("ho2", [P, NDC * T], BF16)
    a_buf = asb("a_buf", [P, TW], BF16)
    i_buf = asb("i_buf", [P, TW], BF16)
    scan_t = asb("scan_t", [P, TW], BF16)
    h2b = asb("h2b", [P, T], BF16)
    wcol = [asb(f"wcol{i}", [P, D], BF16) for i in range(4)]
    sig_i = [asb(f"sig_i{i}", [P, 512], BF16) for i in range(2)]
    sig_nf = [asb(f"sig_nf{i}", [P, 512], BF16) for i in range(2)]
    i1_t = asb("i1_t", [P, 512], F32)
    r_bc = asb("r_bc", [P, T], F32)
    sgt = [asb(f"sgt{i}", [P, 512], BF16) for i in range(2)]
    ggt = asb("ggt", [P, 512], F32)
    hgt = asb("hgt", [P, 512], F32)
    t4t = asb("t4t", [P, 512], F32)
    ao_t = [asb(f"ao_t{i}", [P, 512], BF16) for i in range(2)]

    TS_A1 = [(0, W)] + [(W + k * 512, 512) for k in range(4)]

    def xn_slice(dc, off, wd):
        if off < W:
            return xn_w[:, dc * W + off: dc * W + off + wd]
        o = off - W
        return xn_m[:, dc * T + o: dc * T + o + wd]

    a1_wld = {}
    a1_wlast = {}
    a1_grp = []
    a1_free = {}
    a1_pair_i2 = {}
    a1_alast = {}
    a1_scan = []
    a1_h2b = []
    a1_stat = []

    for ec in range(NDC):
        for kind, mat, slot in (("i", Wi, ec % 2), ("f", Wf, 2 + ec % 2)):
            def g(e, ec=ec, mat=mat, slot=slot, kind=kind):
                if ec == 0 and kind == "i":
                    e.wait_ge(s_ve, P0_VE)
                    e.wait_ge(s_tn, P0_TN)
                if ec >= 2:
                    e.wait_ge(s_tn, a1_wlast[(kind, ec - 2)])
                e.dma_start(
                    out=wcol[slot].rearrange("p (dc e) -> p dc e", e=P),
                    in_=mat.rearrange("(dc p) e -> p dc e", p=P)[:, :, ec * P:(ec + 1) * P],
                ).then_inc(s_gp, 16)
            em("gpsimd", g)
            a1_wld[(kind, ec)] = c_gp.inc(16)

    gidx = 0
    for ec in range(NDC):
        for tsi, (off, wd) in enumerate(TS_A1):
            pidx = ec * len(TS_A1) + tsi
            par = pidx % 2
            ibank, fbank = gidx % 4, (gidx + 1) % 4

            for kind in ("i", "f"):
                bank = gidx % 4
                mywld = a1_wld[(kind, ec)]

                def t(e, ec=ec, off=off, wd=wd, kind=kind, slot=(ec % 2 if kind == "i" else 2 + ec % 2),
                      bank=bank, gidx=gidx, mywld=mywld):
                    e.wait_ge(s_gp, mywld)
                    if gidx == 0:
                        e.wait_ge(s_ve, P0_VE)
                    if gidx >= 4:
                        sem, cval = a1_free[gidx - 4]
                        e.wait_ge(sem, cval)
                    for dc in range(NDC):
                        mm = e.matmul(psb[bank][:, 0:wd],
                                      wcol[slot][:, dc * P:(dc + 1) * P],
                                      xn_slice(dc, off, wd),
                                      start=(dc == 0), stop=(dc == NDC - 1))
                    mm.then_inc(s_tn, 1)
                em("tensor", t)
                a1_grp.append(c_tn.inc())
                a1_wlast[(kind, ec)] = a1_grp[-1]
                gidx += 1

            ig, fg = a1_grp[-2], a1_grp[-1]

            def sc(e, par=par, ibank=ibank, fbank=fbank, ig=ig, fg=fg,
                   ec=ec, tsi=tsi, off=off, wd=wd, pidx=pidx):
                e.wait_ge(s_tn, ig)
                if pidx >= 2:
                    e.wait_ge(s_ve, a1_pair_i2[pidx - 2])
                e.activation(sig_i[par][:, 0:wd], psb[ibank][:, 0:wd],
                             AF.Sigmoid).then_inc(s_se, 1)
                e.wait_ge(s_tn, fg)
                if ec > 0 and tsi == 0:
                    e.wait_ge(s_ve, a1_scan[ec - 1])
                e.activation(a_buf[:, off:off + wd], psb[fbank][:, 0:wd],
                             AF.Sigmoid).then_inc(s_se, 1)
                e.activation(sig_nf[par][:, 0:wd], psb[fbank][:, 0:wd],
                             AF.Sigmoid, scale=-1.0).then_inc(s_se, 1)
            em("scalar", sc)
            se_sigi = c_se.inc()
            se_a = c_se.inc()
            se_signf = c_se.inc()
            a1_alast[ec] = se_a
            a1_free[gidx - 1] = (s_se, se_signf)

            def v(e, par=par, ibank=ibank, off=off, wd=wd, c1=c_ve.v + 1,
                  se_sigi=se_sigi, se_signf=se_signf, ec=ec, tsi=tsi):
                e.wait_ge(s_se, se_sigi)
                if ec > 0 and tsi == 0:
                    e.wait_ge(s_ve, a1_scan[ec - 1])
                e.tensor_tensor(i1_t[:, 0:wd], psb[ibank][:, 0:wd],
                                sig_i[par][:, 0:wd], OP.mult).then_inc(s_ve, 1)
                e.wait_ge(s_se, se_signf)
                e.wait_ge(s_ve, c1)
                e.tensor_tensor(i_buf[:, off:off + wd], i1_t[:, 0:wd],
                                sig_nf[par][:, 0:wd], OP.mult).then_inc(s_ve, 1)
            em("vector", v)
            ve_i1 = c_ve.inc()
            ve_i2 = c_ve.inc()
            a1_free[gidx - 2] = (s_ve, ve_i1)
            a1_pair_i2[pidx] = ve_i2

        def v(e, ec=ec, c1=c_ve.v + 1, c2=c_ve.v + 2, ilast=a1_pair_i2[ec * 5 + 4]):
            e.wait_ge(s_se, a1_alast[ec])
            e.wait_ge(s_ve, ilast)
            e.tensor_tensor_scan(scan_t[:], a_buf[:], i_buf[:], 0.0,
                                 OP.mult, OP.add).then_inc(s_ve, 1)
            e.wait_ge(s_ve, c1)
            e.tensor_copy(ho2[:, ec * T:(ec + 1) * T], scan_t[:, W:TW]).then_inc(s_ve, 1)
            if ec > 0:
                e.wait_ge(s_tn, a1_stat[ec - 1])
            e.wait_ge(s_ve, c2)
            e.tensor_tensor(h2b[:], ho2[:, ec * T:(ec + 1) * T],
                            ho2[:, ec * T:(ec + 1) * T], OP.mult).then_inc(s_ve, 1)
        em("vector", v)
        c_ve.inc()
        a1_scan.append(c_ve.inc())
        a1_h2b.append(c_ve.inc())

        def t(e, ec=ec):
            e.wait_ge(s_ve, a1_h2b[ec])
            for s4 in range(4):
                mm = e.matmul(psb[4 + s4][0:1, :], ones_c[:],
                              h2b[:, s4 * 512:(s4 + 1) * 512],
                              start=(ec == 0), stop=(ec == NDC - 1))
            mm.then_inc(s_tn, 1)
        em("tensor", t)
        a1_stat.append(c_tn.inc())

    A1_TN = c_tn.v

    def v(e):
        e.wait_ge(s_tn, a1_stat[-1])
        for s4 in range(4):
            e.tensor_copy(r_bc[0:1, s4 * 512:(s4 + 1) * 512], psb[4 + s4][0:1, :])
        e.engine_nop().then_inc(s_ve, 1)
    em("vector", v)
    A2_VAR = c_ve.inc()

    a2_bc = []
    for s4 in range(4):
        def t(e, s4=s4):
            e.wait_ge(s_ve, A2_VAR)
            e.matmul(psb[s4][:], ones_r[:], r_bc[0:1, s4 * 512:(s4 + 1) * 512],
                     start=True, stop=True).then_inc(s_tn, 1)
        em("tensor", t)
        a2_bc.append(c_tn.inc())

    a2_vs, a2_ss = [], []
    for s4 in range(4):
        def v(e, s4=s4, c1=c_ve.v + 1):
            e.wait_ge(s_tn, a2_bc[s4])
            sl = slice(s4 * 512, (s4 + 1) * 512)
            e.tensor_copy(r_bc[:, sl], psb[s4][:]).then_inc(s_ve, 1)
            e.wait_ge(s_ve, c1)
            e.tensor_scalar(ggt[:], r_bc[:, sl], 1.0 / D, EPS,
                            OP.mult, OP.add).then_inc(s_ve, 1)
        em("vector", v)
        c_ve.inc()
        a2_vs.append(c_ve.inc())

        def sc(e, s4=s4, v_=a2_vs[-1], c1=c_se.v + 1):
            e.wait_ge(s_ve, v_)
            sl = slice(s4 * 512, (s4 + 1) * 512)
            e.activation(hgt[:], ggt[:], AF.Ln).then_inc(s_se, 1)
            e.wait_ge(s_se, c1)
            e.activation(r_bc[:, sl], hgt[:], AF.Exp, scale=-0.5).then_inc(s_se, 1)
        em("scalar", sc)
        c_se.inc()
        a2_ss.append(c_se.inc())

        if s4 < 3:
            def v2(e, s_=a2_ss[-1]):
                e.wait_ge(s_se, s_)
            em("vector", v2)

    def v(e):
        e.wait_ge(s_se, a2_ss[-1])
        e.engine_nop().then_inc(s_ve, 1)
    em("vector", v)
    A2_RBC = c_ve.inc()

    a3_wld, a3_wlast = {}, {}
    a3_grp, a3_free = [], []
    for ec in range(NDC):
        def g(e, ec=ec):
            if ec < 2:
                e.wait_ge(s_tn, A1_TN)
            else:
                e.wait_ge(s_tn, a3_wlast[ec - 2])
            e.dma_start(
                out=wcol[ec % 2].rearrange("p (dc e) -> p dc e", e=P),
                in_=Wg.rearrange("(dc p) e -> p dc e", p=P)[:, :, ec * P:(ec + 1) * P],
            ).then_inc(s_gp, 16)
        em("gpsimd", g)
        a3_wld[ec] = c_gp.inc(16)

    gidx = 0
    for ec in range(NDC):
        for tsi in range(4):
            bank = gidx % 4
            par = gidx % 2

            def t(e, ec=ec, tsi=tsi, bank=bank, gidx=gidx):
                e.wait_ge(s_gp, a3_wld[ec])
                if gidx == 0:
                    e.wait_ge(s_ve, A2_RBC)
                if gidx >= 4:
                    e.wait_ge(s_ve, a3_free[gidx - 4])
                for dc in range(NDC):
                    mm = e.matmul(psb[bank][:],
                                  wcol[ec % 2][:, dc * P:(dc + 1) * P],
                                  xn_m[:, dc * T + tsi * 512: dc * T + (tsi + 1) * 512],
                                  start=(dc == 0), stop=(dc == NDC - 1))
                mm.then_inc(s_tn, 1)
            em("tensor", t)
            a3_grp.append(c_tn.inc())
            a3_wlast[ec] = a3_grp[-1]

            def sc(e, bank=bank, par=par, g_=a3_grp[-1], gidx=gidx):
                e.wait_ge(s_tn, g_)
                if gidx >= 2:
                    e.wait_ge(s_ve, a3_free[gidx - 2])
                e.activation(sgt[par][:], psb[bank][:], AF.Sigmoid).then_inc(s_se, 1)
            em("scalar", sc)
            se_sg = c_se.inc()

            def v(e, ec=ec, tsi=tsi, bank=bank, par=par, se_sg=se_sg):
                e.wait_ge(s_se, se_sg)
                e.tensor_tensor(ggt[:], psb[bank][:], sgt[par][:], OP.mult).then_inc(s_ve, 1)
                sl = slice(ec * T + tsi * 512, ec * T + (tsi + 1) * 512)
                e.tensor_scalar(hgt[:], ho2[:, sl], gam[:, ec:ec + 1], None, OP.mult)
                e.tensor_tensor(t4t[:], hgt[:], ggt[:], OP.mult)
                e.tensor_tensor(ho2[:, sl], t4t[:],
                                r_bc[:, tsi * 512:(tsi + 1) * 512], OP.mult).then_inc(s_ve, 1)
            em("vector", v)
            c_ve.inc()
            a3_free.append(c_ve.inc())
            gidx += 1
            qe.emit(1)

    A3_TN = c_tn.v

    a4_wld, a4_wlast = {}, {}
    a4_grp, a4_free, a4_stw = [], [], []
    for dco in range(NDC):
        def g(e, dco=dco):
            if dco < 2:
                e.wait_ge(s_tn, A1_TN)
            else:
                e.wait_ge(s_tn, a4_wlast[dco - 2])
            e.dma_start(
                out=wcol[2 + dco % 2].rearrange("p (dc e) -> p dc e", e=P),
                in_=Wo.rearrange("(ec p) d -> p ec d", p=P)[:, :, dco * P:(dco + 1) * P],
            ).then_inc(s_gp, 16)
        em("gpsimd", g)
        a4_wld[dco] = c_gp.inc(16)

    gidx = 0
    for dco in range(NDC):
        for tsi in range(4):
            bank = gidx % 8
            par = gidx % 2

            def t(e, dco=dco, tsi=tsi, bank=bank, gidx=gidx):
                e.wait_ge(s_gp, a4_wld[dco])
                if gidx == 0:
                    e.wait_ge(s_ve, a3_free[-1])
                if gidx >= 8:
                    e.wait_ge(s_ve, a4_free[gidx - 8])
                for ec in range(NDC):
                    mm = e.matmul(psb[bank][:],
                                  wcol[2 + dco % 2][:, ec * P:(ec + 1) * P],
                                  ho2[:, ec * T + tsi * 512: ec * T + (tsi + 1) * 512],
                                  start=(ec == 0), stop=(ec == NDC - 1))
                mm.then_inc(s_tn, 1)
            em("tensor", t)
            a4_grp.append(c_tn.inc())
            a4_wlast[dco] = a4_grp[-1]

            def v(e, bank=bank, par=par, g_=a4_grp[-1], gidx=gidx):
                e.wait_ge(s_tn, g_)
                if gidx >= 2:
                    e.wait_ge(s_sy, a4_stw[gidx - 2])
                e.tensor_copy(ao_t[par][:], psb[bank][:]).then_inc(s_ve, 1)
            em("vector", v)
            a4_free.append(c_ve.inc())

            def s(e, dco=dco, tsi=tsi, par=par, v_=a4_free[-1]):
                e.wait_ge(s_ve, v_)
                e.dma_start(out=scr_ao[dco * P:(dco + 1) * P, tsi * 512:(tsi + 1) * 512],
                            in_=ao_t[par][:]).then_inc(s_sy, 16)
            em("sync", s)
            a4_stw.append(c_sy.inc(16))
            gidx += 1
            qe.emit(2)

    A4_TN = c_tn.v
    A4_VE = c_ve.v

    qe.finish()
    QWR_LAST = q_wr[NQ - 1]

    es_a.close()
    es_f = ExitStack()
    fsb = lambda n, s, d: es_f.enter_context(nc.sbuf_tensor(n, s, d))
    fuse = fsb("fuse", [P, NIC * 512], BF16)
    gcol = [fsb(f"gcol{i}", [P, GUB * D], BF16) for i in range(2)]
    ucol = [fsb(f"ucol{i}", [P, GUB * D], BF16) for i in range(2)]
    dwcol = [fsb(f"dwcol{i}", [P, FI], BF16) for i in range(2)]
    gyt = [fsb(f"gyt{i}", [P, 512], F32) for i in range(2)]
    uyt = fsb("uyt", [P, 512], F32)
    sgy = [fsb(f"sgy{i}", [P, 512], BF16) for i in range(2)]
    g2t = fsb("g2t", [P, 512], F32)
    s_tl = [fsb(f"s_tl{i}", [P, 512], BF16) for i in range(2)]

    NB = NIC // GUB
    f_bld = {}
    f_blast = {}
    f_grp = []
    f_gfree = []
    f_sgdone = []
    f_fusedone = []
    f_dld = {}
    f_dlast = {}
    f_dgrp, f_dfree, f_dsw = [], [], []

    gflat = 0
    for q in range(4):
        qt = q * 512
        for b in range(NB):
            bq = q * NB + b
            for tens, scr, tname in ((gcol, scr_g, "g"), (ucol, scr_u, "u")):
                def g(e, q=q, b=b, bq=bq, tens=tens, scr=scr, tname=tname):
                    if bq == 0 and tname == "g":
                        e.wait_ge(s_tn, A4_TN)
                        e.wait_ge(s_ve, A4_VE)
                        e.wait_ge(s_sy, a4_stw[-1])
                        e.wait_ge(s_sy, QWR_LAST)
                    if bq >= 2:
                        e.wait_ge(s_tn, f_blast[(tname, bq - 2)])
                    e.dma_start(
                        out=tens[b % 2].rearrange("p (ic f) -> p ic f", f=D),
                        in_=scr[GUB * b:GUB * (b + 1)].rearrange("ic p f -> p ic f"),
                    ).then_inc(s_gp, 16)
                em("gpsimd", g)
                f_bld[(tname, bq)] = c_gp.inc(16)

        for ic in range(NIC):
            b = ic // GUB
            bq = q * NB + b
            icq = q * NIC + ic
            for kind, tens, tname in (("g", gcol, "g"), ("u", ucol, "u")):
                bank = gflat % 4

                def t(e, q=q, qt=qt, ic=ic, b=b, bq=bq, tens=tens, tname=tname,
                      bank=bank, gflat=gflat):
                    e.wait_ge(s_gp, f_bld[(tname, bq)])
                    if gflat < 4:
                        e.wait_ge(s_ve, A4_VE)
                    else:
                        e.wait_ge(s_ve, f_gfree[gflat - 4])
                    for dc in range(NDC):
                        mm = e.matmul(psb[bank][:],
                                      tens[b % 2][:, (ic % GUB) * D + dc * P:
                                                   (ic % GUB) * D + (dc + 1) * P],
                                      xn_m[:, dc * T + qt: dc * T + qt + 512],
                                      start=(dc == 0), stop=(dc == NDC - 1))
                    mm.then_inc(s_tn, 1)
                em("tensor", t)
                f_grp.append(c_tn.inc())
                f_blast[(tname, bq)] = f_grp[-1]
                gflat += 1

            gg, ug = f_grp[-2], f_grp[-1]
            gbank, ubank = (gflat - 2) % 4, (gflat - 1) % 4
            par = icq % 2

            def v1(e, ic=ic, icq=icq, par=par, gg=gg, gbank=gbank):
                e.wait_ge(s_tn, gg)
                if icq >= 2:
                    e.wait_ge(s_se, f_sgdone[icq - 2])
                e.tensor_scalar(gyt[par][:], psb[gbank][:], gssb[:, ic:ic + 1],
                                None, OP.mult).then_inc(s_ve, 1)
            em("vector", v1)
            gy_done = c_ve.inc()
            f_gfree.append(gy_done)

            def sc(e, par=par, gy_done=gy_done):
                e.wait_ge(s_ve, gy_done)
                e.activation(sgy[par][:], gyt[par][:], AF.Sigmoid).then_inc(s_se, 1)
            em("scalar", sc)
            f_sgdone.append(c_se.inc())

            def v2(e, ic=ic, par=par, ug=ug, ubank=ubank, sg=f_sgdone[-1],
                   c2=c_ve.v + 2):
                e.wait_ge(s_tn, ug)
                e.tensor_scalar(uyt[:], psb[ubank][:], ussb[:, ic:ic + 1],
                                None, OP.mult).then_inc(s_ve, 1)
                e.wait_ge(s_se, sg)
                e.tensor_tensor(g2t[:], gyt[par][:], sgy[par][:], OP.mult).then_inc(s_ve, 1)
                e.wait_ge(s_ve, c2)
                e.tensor_tensor(fuse[:, ic * 512:(ic + 1) * 512], g2t[:],
                                uyt[:], OP.mult).then_inc(s_ve, 1)
            em("vector", v2)
            f_gfree.append(c_ve.inc())
            c_ve.inc()
            c_ve.inc()
        f_fusedone.append(c_ve.v)

        for hc in range(NDC):
            di = q * NDC + hc

            def g(e, q=q, hc=hc, di=di):
                if di < 2:
                    e.wait_ge(s_ve, A4_VE)
                    e.wait_ge(s_sy, QWR_LAST)
                else:
                    e.wait_ge(s_tn, f_dlast[di - 2])
                e.dma_start(out=dwcol[hc % 2][:], in_=scr_d[hc]).then_inc(s_gp, 16)
            em("gpsimd", g)
            f_dld[di] = c_gp.inc(16)

            def t(e, q=q, hc=hc, di=di, bank=4 + di % 2):
                e.wait_ge(s_gp, f_dld[di])
                if hc == 0:
                    e.wait_ge(s_ve, f_fusedone[q])
                if di < 2:
                    e.wait_ge(s_ve, A4_VE)
                else:
                    e.wait_ge(s_ve, f_dfree[di - 2])
                for ic in range(NIC):
                    mm = e.matmul(psb[bank][:],
                                  dwcol[hc % 2][:, ic * P:(ic + 1) * P],
                                  fuse[:, ic * 512:(ic + 1) * 512],
                                  start=(ic == 0), stop=(ic == NIC - 1))
                mm.then_inc(s_tn, 1)
            em("tensor", t)
            f_dgrp.append(c_tn.inc())
            f_dlast[di] = f_dgrp[-1]

            def v(e, hc=hc, di=di, bank=4 + di % 2, par=di % 2, g_=f_dgrp[-1]):
                e.wait_ge(s_tn, g_)
                if di >= 2:
                    e.wait_ge(s_sy, f_dsw[di - 2])
                e.tensor_scalar(s_tl[par][:], psb[bank][:], dssb[:, hc:hc + 1],
                                None, OP.mult).then_inc(s_ve, 1)
            em("vector", v)
            f_dfree.append(c_ve.inc())

            def s(e, q=q, qt=qt, hc=hc, par=di % 2, v_=f_dfree[-1]):
                e.wait_ge(s_ve, v_)
                e.dma_start(out=scr_s[hc * P:(hc + 1) * P, qt:qt + 512],
                            in_=s_tl[par][:]).then_inc(s_sy, 16)
            em("sync", s)
            f_dsw.append(c_sy.inc(16))

    F_TN = c_tn.v
    F_VE = c_ve.v
    F_SW = f_dsw[-1]

    es_f.close()
    es_fin = ExitStack()
    zsb = lambda n, s, d: es_fin.enter_context(nc.sbuf_tensor(n, s, d))
    s_col = [zsb(f"s_col{i}", [P, D], BF16) for i in range(2)]
    ao_col = [zsb(f"ao_col{i}", [P, D], BF16) for i in range(2)]
    x_t2 = [zsb(f"x_t2{i}", [P, D], F32) for i in range(2)]
    comb = [zsb(f"comb{i}", [P, D], F32) for i in range(2)]
    o_tl = [zsb(f"o_tl{i}", [P, D], F32) for i in range(2)]
    o_fin = [zsb(f"o_fin{i}", [P, D], F32) for i in range(2)]

    scr_s_v = scr_s.rearrange("(hc p) t -> p hc t", p=P)
    scr_ao_v = scr_ao.rearrange("(hc p) t -> p hc t", p=P)

    z_ld, z_cb, z_tr, z_ev, z_fin, z_stw = [], [], [], [], [], []
    for tg in range(NTGM):
        par = tg % 2

        def s(e, tg=tg, par=par):
            if tg == 0:
                e.wait_ge(s_tn, F_TN)
                e.wait_ge(s_ve, F_VE)
                e.wait_ge(s_sy, F_SW)
            if tg >= 2:
                e.wait_ge(s_ve, z_cb[tg - 2])
                e.wait_ge(s_ve, z_fin[tg - 2])
            e.dma_start(out=s_col[par].rearrange("p (hc t) -> p hc t", t=P),
                        in_=scr_s_v[:, :, tg * P:(tg + 1) * P]).then_inc(s_sy, 16)
            e.dma_start(out=ao_col[par].rearrange("p (hc t) -> p hc t", t=P),
                        in_=scr_ao_v[:, :, tg * P:(tg + 1) * P]).then_inc(s_sy, 16)
            e.dma_start(out=x_t2[par][:],
                        in_=x_in[W + tg * P: W + (tg + 1) * P, :]).then_inc(s_sy, 16)
        em("sync", s)
        c_sy.inc(32)
        z_ld.append(c_sy.inc(16))

        def v(e, tg=tg, par=par):
            e.wait_ge(s_sy, z_ld[tg])
            if tg >= 2:
                e.wait_ge(s_tn, z_tr[(tg - 1) * NDC - 1])
            e.tensor_tensor(comb[par][:], s_col[par][:], ao_col[par][:],
                            OP.add).then_inc(s_ve, 1)
        em("vector", v)
        z_cb.append(c_ve.inc())

        for hc in range(NDC):
            gi = tg * NDC + hc
            bank = gi % 4

            def t(e, tg=tg, hc=hc, par=par, bank=bank, gi=gi):
                e.wait_ge(s_ve, z_cb[tg])
                if gi >= 4:
                    e.wait_ge(s_ve, z_ev[gi - 4])
                e.transpose(psb[bank][:, 0:P], comb[par][:, hc * P:(hc + 1) * P],
                            ident[:]).then_inc(s_tn, 1)
            em("tensor", t)
            z_tr.append(c_tn.inc())

            def v(e, hc=hc, par=par, bank=bank, gi=gi):
                e.wait_ge(s_tn, z_tr[gi])
                e.tensor_copy(o_tl[par][:, hc * P:(hc + 1) * P],
                              psb[bank][:, 0:P]).then_inc(s_ve, 1)
            em("vector", v)
            z_ev.append(c_ve.inc())

        def v(e, tg=tg, par=par, lastev=z_ev[-1]):
            if tg >= 2:
                e.wait_ge(s_sy, z_stw[tg - 2])
            e.wait_ge(s_ve, lastev)
            e.tensor_tensor(o_fin[par][:], o_tl[par][:], x_t2[par][:],
                            OP.add).then_inc(s_ve, 1)
        em("vector", v)
        z_fin.append(c_ve.inc())

        def s(e, tg=tg, par=par, v_=z_fin[-1]):
            e.wait_ge(s_ve, v_)
            e.dma_start(out=out_d[tg * P:(tg + 1) * P, :],
                        in_=o_fin[par][:]).then_inc(s_sy, 16)
        em("sync", s)
        z_stw.append(c_sy.inc(16))

    def s(e):
        e.wait_ge(s_sy, z_stw[-1])
    em("sync", s)

    with nc.Block() as block:
        @block.sync
        def _(e):
            for f in ops["sync"]:
                f(e)

        @block.gpsimd
        def _(e):
            for f in ops["gpsimd"]:
                f(e)

        @block.tensor
        def _(e):
            for f in ops["tensor"]:
                f(e)

        @block.vector
        def _(e):
            for f in ops["vector"]:
                f(e)

        @block.scalar
        def _(e):
            for f in ops["scalar"]:
                f(e)

    es_fin.close()
    es_q.close()
    es_top.close()
    return nc


def kernel(L0_attn_hidden_states, L0_attn_Wi, L0_attn_Wf, L0_attn_Wg, L0_attn_Wo,
           L0_attn_g_norm_weight, L0_ffn_gate_w, L0_ffn_up_w, L0_ffn_down_w,
           L0_ffn_gate_scale, L0_ffn_up_scale, L0_ffn_down_scale):
    x = np.ascontiguousarray(np.asarray(L0_attn_hidden_states, dtype=np.float32)[0])
    n = 8
    shards = []
    for c in range(n):
        lo = c * T - W
        if lo < 0:
            pad = np.zeros((W, D), dtype=np.float32)
            shards.append(np.concatenate([pad, x[0:(c + 1) * T]], axis=0))
        else:
            shards.append(np.ascontiguousarray(x[lo:(c + 1) * T]))

    wmap = dict(
        Wi=np.ascontiguousarray(L0_attn_Wi, dtype=np.float32),
        Wf=np.ascontiguousarray(L0_attn_Wf, dtype=np.float32),
        Wg=np.ascontiguousarray(L0_attn_Wg, dtype=np.float32),
        Wo=np.ascontiguousarray(L0_attn_Wo, dtype=np.float32),
        gnw=np.ascontiguousarray(np.asarray(L0_attn_g_norm_weight, dtype=np.float32).reshape(NDC, P).T),
        gw=np.ascontiguousarray(L0_ffn_gate_w, dtype=np.float32),
        uw=np.ascontiguousarray(L0_ffn_up_w, dtype=np.float32),
        dw=np.ascontiguousarray(L0_ffn_down_w, dtype=np.float32),
        gs=np.ascontiguousarray(np.asarray(L0_ffn_gate_scale, dtype=np.float32).reshape(NIC, P).T),
        us=np.ascontiguousarray(np.asarray(L0_ffn_up_scale, dtype=np.float32).reshape(NIC, P).T),
        ds=np.ascontiguousarray(np.asarray(L0_ffn_down_scale, dtype=np.float32).reshape(NDC, P).T),
    )
    in_maps = [dict(x_sh=shards[c], **wmap) for c in range(n)]

    nc = build_program()
    res = run_bass_kernel_spmd(nc, in_maps, core_ids=list(range(n)))
    out = np.concatenate([res.results[c]["out_sh"] for c in range(n)], axis=0)
    return out[None, :, :]
